# version 02
# baseline (speedup 1.0000x reference)
import math
import numpy as np
import ml_dtypes
from contextlib import ExitStack

import concourse.bass as bass
import concourse.mybir as mybir
from concourse.bass_utils import run_bass_kernel_spmd

F32 = mybir.dt.float32
BF16 = mybir.dt.bfloat16
I32 = mybir.dt.int32
AF = mybir.ActivationFunctionType
ALU = mybir.AluOpType
AX = mybir.AxisListType

D = 2048
NDC = 16
TOK = 2048
HALF = 1024
TT = 512
IMOE = 1408
NIC = 11
NEXP = 16
SEQ = 8192
NQH = 4
RS = 1.0 / math.sqrt(128.0)
MOFF = 16.0
EPS = 1e-5
TWO_PI = 2.0 * math.pi
C1 = float(round(TWO_PI * 2 ** 10) / 2 ** 10)
C2 = float(np.float32(TWO_PI - C1))
C3 = float(np.float32(TWO_PI - C1 - np.float64(np.float32(TWO_PI - C1))))
LN1E4 = math.log(10000.0)

ENGINES = ("sync", "tensor", "vector", "scalar", "gpsimd")
ALLSEMS = ("cc", "mm", "ve", "ac", "gp",
           "xi", "xg", "xa", "xc", "xst", "wq", "wo", "oc",
           "cr0", "cr1", "wg0", "wg1", "wd0", "wd1",
           "xt0", "xt1", "ht0", "ht1", "hg0", "hg1",
           "og0", "og1", "x60", "x61", "xo0", "xo1")


class Prog:
    def __init__(self, nc, stack):
        self.nc = nc
        self.items = {e: [] for e in ENGINES}
        self.sems = {}
        self.cnt = {}
        self.stack = stack
        self.nbar = 0

    def sem(self, name):
        self.sems[name] = self.stack.enter_context(self.nc.semaphore(name))
        self.cnt[name] = 0

    def emit(self, eng, fn):
        self.items[eng].append(fn)

    def inc(self, eng, sem, fn, n=1):
        self.cnt[sem] += n
        tgt = self.cnt[sem]
        s = self.sems[sem]

        def run(e, fn=fn, s=s, n=n):
            fn(e).then_inc(s, n)
        self.items[eng].append(run)
        return tgt

    def wait(self, eng, sem, val):
        if val <= 0:
            return
        assert val <= self.cnt[sem], f"wait {sem}>={val} planned={self.cnt[sem]}"
        s = self.sems[sem]
        self.items[eng].append(lambda e, s=s, v=val: e.wait_ge(s, v))

    def barrier(self):
        waits = [(s, self.cnt[s]) for s in ALLSEMS]
        self.nbar += 1
        tgt = 5 * self.nbar
        bs = self.sems["bar"]
        for eng in ENGINES:
            for s, v in waits:
                self.wait(eng, s, v)

            def run(e, bs=bs, tgt=tgt):
                e.nop().then_inc(bs, 1)
                e.wait_ge(bs, tgt)
            self.items[eng].append(run)
        self.cnt["bar"] = tgt

    def run(self):
        with self.nc.Block() as blk:
            for eng, starter in (("gpsimd", blk.gpsimd), ("scalar", blk.scalar),
                                 ("tensor", blk.tensor), ("vector", blk.vector),
                                 ("sync", blk.sync)):
                items = self.items[eng]

                def body(e, items=items):
                    for fn in items:
                        fn(e)
                starter(body)


def build_program(debug_taps=(), phases=("ffn1", "attn", "ffn2")):
    nc = bass.Bass("TRN2", target_bir_lowering=False, debug=False)
    stack = ExitStack()

    def kind_of(name):
        return "ExternalOutput" if name in debug_taps else "Internal"

    x_in = nc.dram_tensor("x_fm", [D, TOK], F32, kind="ExternalInput").ap()
    wgu_r = nc.dram_tensor("wgu_r", [NEXP, NIC, 128, 4096], F32, kind="ExternalInput").ap()
    wd_r = nc.dram_tensor("wd_r", [NEXP, NDC, 128, IMOE], F32, kind="ExternalInput").ap()
    shgu_r = nc.dram_tensor("shgu_r", [44, 128, 4096], F32, kind="ExternalInput").ap()
    shd_r = nc.dram_tensor("shd_r", [NDC, 128, 5632], F32, kind="ExternalInput").ap()
    rt_r = nc.dram_tensor("rt_r", [128, 256], F32, kind="ExternalInput").ap()
    sgw_r = nc.dram_tensor("sgw_r", [128, 16], F32, kind="ExternalInput").ap()
    wq_my = nc.dram_tensor("wq_my", [NQH, 128, D], F32, kind="ExternalInput").ap()
    wk_my = nc.dram_tensor("wk_my", [128, D], F32, kind="ExternalInput").ap()
    wv_my = nc.dram_tensor("wv_my", [128, D], F32, kind="ExternalInput").ap()
    wo_r = nc.dram_tensor("wo_r", [NDC, 128, D], F32, kind="ExternalInput").ap()
    hs_my = nc.dram_tensor("hs_my", [1, NQH], F32, kind="ExternalInput").ap()
    qsel = nc.dram_tensor("qsel", [1, 1], I32, kind="ExternalInput").ap()

    x1 = nc.dram_tensor("x1", [D, TOK], F32, kind=kind_of("x1")).ap()
    x2 = nc.dram_tensor("x2", [D, TOK], F32, kind=kind_of("x2")).ap()
    y = nc.dram_tensor("y", [D, TOK], F32, kind="ExternalOutput").ap()
    comb_d = nc.dram_tensor("comb_d", [NEXP, HALF], F32, kind=kind_of("comb_d")).ap()
    hc_d = nc.dram_tensor("hc_d", [D, TOK], BF16, kind="Internal").ap()
    hg_d = nc.dram_tensor("hg_d", [4 * D, TOK], BF16, kind=kind_of("hg_d")).ap()
    oc_d = nc.dram_tensor("oc_d", [NQH * 128, SEQ], BF16, kind="Internal").ap()
    og_d = nc.dram_tensor("og_d", [4 * NQH * 128, SEQ], BF16, kind=kind_of("og_d")).ap()
    lrow_d = nc.dram_tensor("lrow_d", [2, TT], F32, kind="Internal").ap()

    A_t = nc.alloc_sbuf_tensor("A", [128, 16384], F32)
    A = A_t.ap()
    arQ = A_t.bitcast(BF16).ap()
    B1 = nc.alloc_sbuf_tensor("B1", [128, 16384], BF16).ap()
    B2 = nc.alloc_sbuf_tensor("B2", [128, 16384], BF16).ap()
    B3 = nc.alloc_sbuf_tensor("B3", [128, 8192], BF16).ap()
    B4 = nc.alloc_sbuf_tensor("B4", [128, 8192], BF16).ap()
    F2_t = nc.alloc_sbuf_tensor("F2", [128, 6144], F32)
    F2 = F2_t.ap()
    F2b = F2_t.bitcast(BF16).ap()
    RW_t = nc.alloc_sbuf_tensor("RW", [128, 2688], F32)
    RW = RW_t.ap()
    RWi = RW_t.bitcast(I32).ap()
    rt_sb = nc.alloc_sbuf_tensor("rt_sb", [128, 256], F32).ap()
    sgw_sb = nc.alloc_sbuf_tensor("sgw_sb", [128, 16], BF16).ap()
    hs_sb = nc.alloc_sbuf_tensor("hs_sb", [1, NQH], F32).ap()
    qsel_sb = nc.alloc_sbuf_tensor("qsel_sb", [1, 1], I32).ap()
    onesc = nc.alloc_sbuf_tensor("onesc", [128, 1], F32).ap()
    onesb = nc.alloc_sbuf_tensor("onesb", [128, 1], BF16).ap()
    onesr = nc.alloc_sbuf_tensor("onesr", [1, 128], F32).ap()
    identb = nc.alloc_sbuf_tensor("identb", [128, 128], BF16).ap()
    identf = nc.alloc_sbuf_tensor("identf", [128, 128], F32).ap()
    ebias = nc.alloc_sbuf_tensor("ebias", [128, 1], F32).ap()
    epsc = nc.alloc_sbuf_tensor("epsc", [1, 1], F32).ap()
    invfq = nc.alloc_sbuf_tensor("invfq", [1, 64], F32).ap()

    h_sb = B1
    act_sb = B2
    wgu_buf = [B3[:, 0:4096], B3[:, 4096:8192]]
    wd_buf = [B4[:, 0:IMOE], B4[:, IMOE:2 * IMOE]]
    silu_scr = [B4[:, 2816 + i * 512: 2816 + (i + 1) * 512] for i in range(2)]
    au_scr = [B4[:, 3840 + i * 512: 3840 + (i + 1) * 512] for i in range(2)]
    comb_bc = [F2[:, 0:1024], F2[:, 1024:2048]]
    sg_bc = F2[:, 2048:3072]
    t1_scr = [F2[:, 3072 + i * 512: 3072 + (i + 1) * 512] for i in range(2)]
    sq_scr = [F2[:, 4096 + i * 512: 4096 + (i + 1) * 512] for i in range(2)]
    K_sb = B2[:, 0:8192]
    hg_buf = [B2[:, 8192:16384], B4[:, 0:8192]]
    V_sb = B3[:, 0:8192]
    On_sb = B1[:, 0:8192]
    pprime = [B1[:, 8192 + i * 512: 8192 + (i + 1) * 512] for i in range(4)]
    vf_scr = B1[:, 12288:12800]
    og_buf = [B1[:, 0:8192], B1[:, 8192:16384]]
    cos_t = F2[:, 0:512]
    sin_t = F2[:, 512:1024]
    rope_a = F2[:, 1024:1536]
    rope_b = F2[:, 1536:2048]
    obc_sb = [F2[:, 2048 + i * 512: 2048 + (i + 1) * 512] for i in range(2)]
    x1_t6 = [F2[:, 3072 + i * 512: 3072 + (i + 1) * 512] for i in range(2)]
    xo_t6 = [F2[:, 4096 + i * 512: 4096 + (i + 1) * 512] for i in range(2)]
    wqkv = [B1[:, i * 2048:(i + 1) * 2048] for i in range(6)]

    m_row = RW[0:1, 0:512]
    v_row = RW[0:1, 512:1024]
    rstd_row = RW[0:1, 1024:1536]
    msq_row = RW[0:1, 1536:2048]
    comb_stg = RW[0:16, 2048:2560]
    lg_s = RW[:, 2560:2576]
    vwork = RW[:, 2576:2592]
    elx = RW[:, 2592:2608]
    selx = RW[:, 2608:2624]
    nmax_c = RW[:, 2624:2625]
    t4n_c = RW[:, 2625:2626]
    s4_c = RW[:, 2626:2627]
    inv_c = RW[:, 2627:2628]
    rstd_cols = RW[:, 2628:2632]
    theta_unused = None
    kf_scr = RW[0:64, 512:1024]
    ki_scr = RWi[0:64, 512:1024]
    r_scr = RW[0:64, 1024:1536]
    rc_scr = RW[0:64, 1536:2048]
    tpos = RW[0:1, 2048:2560]
    invl_row = [RW[0:1, 0:512], RW[0:1, 512:1024]]
    crow = [RW[0:1, 0:1024], RW[0:1, 1024:2048]]

    pb = [nc.alloc_psum_tensor(f"pb{i}", [128, 512], F32).ap() for i in range(7)]
    tpb = nc.alloc_psum_tensor("tpb", [128, 1024], BF16).ap()

    P = Prog(nc, stack)
    P.sem("bar")
    for s in ALLSEMS:
        P.sem(s)

    brel = {}
    bmm = {}

    def mmgrp(bk, emits, extra_waits=()):
        if bk in brel:
            P.wait("tensor", *brel.pop(bk))
        for s, v in extra_waits:
            P.wait("tensor", s, v)
        n = len(emits)
        t = None
        for i, fn in enumerate(emits):
            st, sp = (i == 0), (i == n - 1)
            if sp:
                t = P.inc("tensor", "mm", lambda e, fn=fn, st=st, sp=sp: fn(e, st, sp))
            else:
                P.emit("tensor", lambda e, fn=fn, st=st, sp=sp: fn(e, st, sp))
        bmm[bk] = t
        return t

    def bread(bk, eng, sem, fn, extra_waits=()):
        P.wait(eng, "mm", bmm[bk])
        for s, v in extra_waits:
            P.wait(eng, s, v)
        t = P.inc(eng, sem, fn)
        brel[bk] = (sem, t)
        return t

    def vstep(fn, wait_self=True):
        if wait_self:
            P.wait("vector", "ve", P.cnt["ve"])
        return P.inc("vector", "ve", fn)

    P.inc("sync", "xi", lambda e: e.dma_start(out=rt_sb, in_=rt_r), 16)
    P.inc("sync", "xi", lambda e: e.dma_start(out=hs_sb, in_=hs_my), 16)
    P.inc("sync", "xi", lambda e: e.dma_start(out=qsel_sb, in_=qsel), 16)
    P.inc("gpsimd", "xg", lambda e: e.dma_start(out=sgw_sb, in_=sgw_r), 16)
    for ap_, val in ((onesc, 1.0), (onesb, 1.0), (onesr, 1.0), (ebias, -MOFF), (epsc, EPS)):
        P.emit("vector", lambda e, ap_=ap_, val=val: e.memset(ap_, val))
    P.inc("vector", "ve", lambda e: e.memset(RW[:, 0:2688], 0.0))
    t_ms = P.inc("gpsimd", "gp", lambda e: e.memset(identb, 0.0))
    P.wait("gpsimd", "gp", t_ms)
    P.inc("gpsimd", "gp", lambda e: e.affine_select(
        out=identb, in_=identb, compare_op=ALU.not_equal, fill=1.0,
        base=0, pattern=[[-1, 128]], channel_multiplier=1))
    t_ms = P.inc("gpsimd", "gp", lambda e: e.memset(identf, 0.0))
    P.wait("gpsimd", "gp", t_ms)
    P.inc("gpsimd", "gp", lambda e: e.affine_select(
        out=identf, in_=identf, compare_op=ALU.not_equal, fill=1.0,
        base=0, pattern=[[-1, 128]], channel_multiplier=1))
    P.inc("gpsimd", "gp", lambda e: e.iota(
        invfq, pattern=[[1, 64]], base=0, channel_multiplier=0,
        allow_small_or_imprecise_dtypes=True))
    P.wait("scalar", "gp", P.cnt["gp"])
    P.inc("scalar", "ac", lambda e: e.activation(invfq, invfq, AF.Exp, scale=-LN1E4 / 64.0))
    P.barrier()

    def ln_tile(adc, h_out, router=False, comb_col0=0):
        sq_rel = [0, 0]
        for dc in range(NDC):
            par = dc % 2
            P.wait("scalar", "mm", sq_rel[par])
            t_ac = P.inc("scalar", "ac", lambda e, dc=dc, par=par: e.activation(
                sq_scr[par], adc(dc), AF.Square))
            if dc == 0:
                for bk in (0, 1):
                    if bk in brel:
                        P.wait("tensor", *brel.pop(bk))
            P.emit("tensor", lambda e, dc=dc: e.matmul(
                pb[0][0:1, :], onesc, adc(dc), start=(dc == 0), stop=(dc == NDC - 1)))
            P.wait("tensor", "ac", t_ac)
            sq_rel[par] = P.inc("tensor", "mm", lambda e, dc=dc, par=par: e.matmul(
                pb[1][0:1, :], onesc, sq_scr[par], start=(dc == 0), stop=(dc == NDC - 1)))
        bmm[0] = bmm[1] = P.cnt["mm"]
        bread(0, "vector", "ve", lambda e: e.tensor_scalar(
            m_row, pb[0][0:1, :], 1.0 / D, None, ALU.mult))
        bread(1, "vector", "ve", lambda e: e.tensor_scalar(
            v_row, pb[1][0:1, :], 1.0 / D, None, ALU.mult))
        vstep(lambda e: e.tensor_mul(msq_row, m_row, m_row))
        t_var = vstep(lambda e: e.tensor_sub(v_row, v_row, msq_row))
        P.wait("scalar", "ve", t_var)
        t_ln = P.inc("scalar", "ac", lambda e: e.activation(rstd_row, v_row, AF.Ln, bias=epsc))
        P.wait("scalar", "ac", t_ln)
        t_rstd = P.inc("scalar", "ac", lambda e: e.activation(
            rstd_row, rstd_row, AF.Exp, scale=-0.5))
        t_ve_rows = P.cnt["ve"]
        mmgrp(2, [lambda e, st, sp: e.matmul(pb[2], onesr, m_row, start=st, stop=sp)],
              extra_waits=[("ve", t_ve_rows)])
        mmgrp(3, [lambda e, st, sp: e.matmul(pb[3], onesr, rstd_row, start=st, stop=sp)],
              extra_waits=[("ac", t_rstd)])
        lgb = [4, 5, 6, 0]
        t1_rel = [0, 0]
        P.wait("vector", "mm", bmm[3])
        last_norm = 0
        for dc in range(NDC):
            par = dc % 2
            if router:
                P.wait("vector", "mm", t1_rel[par])
            P.inc("vector", "ve", lambda e, dc=dc, par=par: e.tensor_sub(
                t1_scr[par], adc(dc), pb[2]))
            P.wait("vector", "ve", P.cnt["ve"])
            t_n = P.inc("vector", "ve", lambda e, dc=dc, par=par: e.tensor_mul(
                h_out(dc), t1_scr[par], pb[3]))
            last_norm = t_n
            if router:
                P.wait("tensor", "ve", t_n)
                for sub in range(4):
                    bk = lgb[sub]
                    if dc == 0 and bk in brel:
                        P.wait("tensor", *brel.pop(bk))
                    fn = (lambda e, dc=dc, par=par, sub=sub, bk=bk: e.matmul(
                        pb[bk][0:128, 0:16], t1_scr[par][:, sub * 128:(sub + 1) * 128],
                        rt_sb[:, dc * 16:(dc + 1) * 16],
                        start=(dc == 0), stop=(dc == NDC - 1)))
                    if sub == 3:
                        t1_rel[par] = P.inc("tensor", "mm", fn)
                    else:
                        P.emit("tensor", fn)
        brel[2] = ("ve", last_norm)
        brel[3] = ("ve", last_norm)
        if not router:
            return last_norm
        for sub in range(4):
            bmm[lgb[sub]] = P.cnt["mm"]
        mmgrp(1, [
            (lambda e, st, sp, sub=sub: e.transpose(
                pb[1][:, sub:sub + 1], rstd_row[0:1, sub * 128:(sub + 1) * 128],
                onesr[0:1, 0:1]))
            for sub in range(4)], extra_waits=[("ac", t_rstd)])
        bread(1, "vector", "ve", lambda e: e.tensor_copy(rstd_cols, pb[1][:, 0:4]))
        for sub in range(4):
            bk = lgb[sub]
            t0 = bread(bk, "vector", "ve", lambda e, bk=bk, sub=sub: e.tensor_scalar(
                lg_s, pb[bk][0:128, 0:16], rstd_cols[:, sub:sub + 1], None, ALU.mult),
                extra_waits=[("ve", P.cnt["ve"]), ("mm", P.cnt["mm"])])
            vstep(lambda e: e.tensor_copy(vwork, lg_s))
            for j in range(3):
                vstep(lambda e: e.reduce_max(nmax_c, vwork, axis=AX.X, negate=True))
                vstep(lambda e: e.tensor_scalar(elx, vwork, nmax_c, None, ALU.add))
                vstep(lambda e: e.tensor_scalar(elx, elx, 0.0, -1e30, ALU.is_ge, ALU.mult))
                vstep(lambda e: e.tensor_add(vwork, vwork, elx))
            t_t4 = vstep(lambda e: e.reduce_max(t4n_c, vwork, axis=AX.X, negate=True))
            P.wait("scalar", "ve", t_t4)
            t_el = P.inc("scalar", "ac", lambda e: e.activation(elx, lg_s, AF.Exp, bias=t4n_c))
            vstep(lambda e: e.tensor_scalar(selx, lg_s, t4n_c, None, ALU.add))
            vstep(lambda e: e.tensor_scalar(selx, selx, 0.0, None, ALU.is_ge))
            P.wait("vector", "ac", t_el)
            vstep(lambda e: e.tensor_mul(elx, elx, selx))
            vstep(lambda e: e.reduce_sum(s4_c, elx, axis=AX.X))
            vstep(lambda e: e.reciprocal(inv_c, s4_c))
            t_cb = vstep(lambda e: e.tensor_scalar(lg_s, elx, inv_c, 0.5, ALU.mult, ALU.mult))
            mmgrp(1, [lambda e, st, sp, sub=sub: e.transpose(
                pb[1][0:16, sub * 128:(sub + 1) * 128], lg_s, identf)],
                extra_waits=[("ve", t_cb)])
        t_stg = bread(1, "vector", "ve", lambda e: e.tensor_copy(comb_stg, pb[1][0:16, 0:512]),
                      extra_waits=[("xc", P.cnt["xc"])])
        P.wait("sync", "ve", t_stg)
        t_cd = P.inc("sync", "xc", lambda e, comb_col0=comb_col0: e.dma_start(
            out=comb_d[:, comb_col0:comb_col0 + 512], in_=comb_stg), 16)
        return t_cd

    def ffn_half(src, dst, half):
        col0 = half * HALF
        src3 = src.rearrange("(dc p) n -> p dc n", p=128)
        A3v = A.rearrange("p (dc n) -> p dc n", n=1024)
        t_A = P.inc("sync", "xa", lambda e: e.dma_start(
            out=A3v, in_=src3[:, :, col0:col0 + HALF]), 16)
        P.wait("scalar", "xa", t_A)
        P.wait("tensor", "xa", t_A)
        P.wait("vector", "xa", t_A)

        def adc(dc, tt):
            return A[:, dc * 1024 + tt * 512: dc * 1024 + tt * 512 + 512]

        comb_done = 0
        for tt in range(2):
            comb_done = ln_tile(
                lambda dc, tt=tt: adc(dc, tt),
                lambda dc, tt=tt: h_sb[:, dc * 1024 + tt * 512: dc * 1024 + tt * 512 + 512],
                router=True, comb_col0=tt * 512)
        t_hdone = P.cnt["ve"]
        for tt in range(2):
            mmgrp(6, [
                (lambda e, st, sp, dc=dc, tt=tt: e.matmul(
                    pb[6][0:1, :], sgw_sb[:, dc:dc + 1],
                    h_sb[:, dc * 1024 + tt * 512: dc * 1024 + tt * 512 + 512],
                    start=st, stop=sp))
                for dc in range(NDC)],
                extra_waits=[("ve", t_hdone), ("xg", P.cnt["xg"])])
            t_sg = bread(6, "scalar", "ac", lambda e: e.activation(
                msq_row, pb[6][0:1, :], AF.Sigmoid))
            P.wait("scalar", "ac", t_sg)
            t_sg2 = P.inc("scalar", "ac", lambda e: e.mul(msq_row, msq_row, 0.5))
            mmgrp(6, [lambda e, st, sp: e.matmul(pb[6], onesr, msq_row, start=st, stop=sp)],
                  extra_waits=[("ac", t_sg2)])
            bread(6, "vector", "ve", lambda e, tt=tt: e.tensor_copy(
                sg_bc[:, tt * 512:(tt + 1) * 512], pb[6]))
        t_sgbc = P.cnt["ve"]

        wgu_rel = [0, 0]
        wd_rel = [0, 0]
        mul_rel = [0, 0]
        cbc_rel = [0, 0]
        crow_rel = [0, 0]
        act_rel = 0
        for xi in range(20):
            is_moe = xi < 16
            if is_moe:
                e_idx = xi
                cpar = xi % 2
                P.wait("gpsimd", "xc", comb_done)
                P.wait("gpsimd", "mm", crow_rel[cpar])
                t_cr = P.inc("gpsimd", f"cr{cpar}", lambda e, e_idx=e_idx, cpar=cpar: e.dma_start(
                    out=crow[cpar], in_=comb_d[e_idx:e_idx + 1, :]), 16)
                for tt in range(2):
                    mmgrp(6, [lambda e, st, sp, cpar=cpar, tt=tt: e.matmul(
                        pb[6], onesr, crow[cpar][0:1, tt * 512:(tt + 1) * 512],
                        start=st, stop=sp)],
                        extra_waits=[(f"cr{cpar}", t_cr), ("ve", cbc_rel[cpar])])
                    t_cp = bread(6, "vector", "ve", lambda e, cpar=cpar, tt=tt: e.tensor_copy(
                        comb_bc[cpar][:, tt * 512:(tt + 1) * 512], pb[6]))
                crow_rel[cpar] = P.cnt["mm"]
                gate = comb_bc[cpar]
                gate_ready = ("ve", t_cp)
            else:
                s_idx = xi - 16
                gate = sg_bc
                gate_ready = ("ve", t_sgbc)
            for ic in range(NIC):
                wpar = (xi * NIC + ic) % 2
                P.wait("gpsimd", "mm", wgu_rel[wpar])
                if is_moe:
                    t_w = P.inc("gpsimd", f"wg{wpar}", lambda e, e_idx=e_idx, ic=ic, wpar=wpar: e.dma_start(
                        out=wgu_buf[wpar], in_=wgu_r[e_idx, ic]), 16)
                else:
                    gic = s_idx * NIC + ic
                    t_w = P.inc("gpsimd", f"wg{wpar}", lambda e, gic=gic, wpar=wpar: e.dma_start(
                        out=wgu_buf[wpar], in_=shgu_r[gic]), 16)
                ew = [(f"wg{wpar}", t_w)]
                if xi == 0 and ic == 0:
                    ew.append(("ve", t_hdone))
                for bk in (0, 1, 2, 3):
                    if bk in brel:
                        P.wait("tensor", *brel.pop(bk))
                for s_, v_ in ew:
                    P.wait("tensor", s_, v_)
                t_u = 0
                for dc in range(NDC):
                    for half_w in range(2):
                        for tt in range(2):
                            bk = half_w * 2 + tt
                            fn = (lambda e, st, sp, dc=dc, wpar=wpar, tt=tt,
                                  bk=bk, hw=half_w: e.matmul(
                                      pb[bk],
                                      wgu_buf[wpar][:, dc * 256 + hw * 128: dc * 256 + hw * 128 + 128],
                                      h_sb[:, dc * 1024 + tt * 512: dc * 1024 + tt * 512 + 512],
                                      start=(dc == 0), stop=(dc == NDC - 1)))
                            if dc == NDC - 1:
                                t_u = P.inc("tensor", "mm", lambda e, fn=fn: fn(e, False, True))
                                bmm[bk] = t_u
                            else:
                                P.emit("tensor", lambda e, fn=fn: fn(e, False, False))
                wgu_rel[wpar] = t_u
                for tt in range(2):
                    gbk, ubk = tt, 2 + tt
                    spar = (ic * 2 + tt) % 2
                    t_si = bread(gbk, "scalar", "ac",
                                 lambda e, spar=spar, gbk=gbk: e.activation(
                                     silu_scr[spar], pb[gbk], AF.Silu),
                                 extra_waits=[("ve", mul_rel[spar])])
                    t_au = bread(ubk, "vector", "ve",
                                 lambda e, spar=spar, ubk=ubk, gate=gate, tt=tt: e.tensor_mul(
                                     au_scr[spar], pb[ubk], gate[:, tt * 512:(tt + 1) * 512]),
                                 extra_waits=[gate_ready, ("ve", mul_rel[spar])])
                    P.wait("vector", "ac", t_si)
                    P.wait("vector", "mm", act_rel)
                    P.wait("vector", "ve", P.cnt["ve"])
                    mul_rel[spar] = P.inc("vector", "ve", lambda e, ic=ic, tt=tt, spar=spar: e.tensor_mul(
                        act_sb[:, ic * 1024 + tt * 512: ic * 1024 + tt * 512 + 512],
                        silu_scr[spar], au_scr[spar]))
            t_act = P.cnt["ve"]
            for oc in range(NDC):
                wpar = (xi * NDC + oc) % 2
                P.wait("gpsimd", "mm", wd_rel[wpar])
                if is_moe:
                    t_w = P.inc("gpsimd", f"wd{wpar}", lambda e, e_idx=e_idx, oc=oc, wpar=wpar: e.dma_start(
                        out=wd_buf[wpar], in_=wd_r[e_idx, oc]), 16)
                else:
                    t_w = P.inc("gpsimd", f"wd{wpar}", lambda e, s_idx=s_idx, oc=oc, wpar=wpar: e.dma_start(
                        out=wd_buf[wpar], in_=shd_r[oc][:, s_idx * IMOE:(s_idx + 1) * IMOE]), 16)
                for bk in (4, 5):
                    if bk in brel:
                        P.wait("tensor", *brel.pop(bk))
                P.wait("tensor", f"wd{wpar}", t_w)
                P.wait("tensor", "ve", t_act)
                t_d = 0
                for ic in range(NIC):
                    for tt in range(2):
                        dbk = 4 + tt
                        fn = (lambda e, st, sp, ic=ic, wpar=wpar, tt=tt, dbk=dbk: e.matmul(
                            pb[dbk], wd_buf[wpar][:, ic * 128:(ic + 1) * 128],
                            act_sb[:, ic * 1024 + tt * 512: ic * 1024 + tt * 512 + 512],
                            start=(ic == 0), stop=(ic == NIC - 1)))
                        if ic == NIC - 1:
                            t_d = P.inc("tensor", "mm", lambda e, fn=fn: fn(e, False, True))
                            bmm[dbk] = t_d
                        else:
                            P.emit("tensor", lambda e, fn=fn: fn(e, False, False))
                wd_rel[wpar] = t_d
                for tt in range(2):
                    dbk = 4 + tt
                    bread(dbk, "vector", "ve", lambda e, oc=oc, tt=tt, dbk=dbk: e.tensor_add(
                        A[:, oc * 1024 + tt * 512: oc * 1024 + tt * 512 + 512],
                        A[:, oc * 1024 + tt * 512: oc * 1024 + tt * 512 + 512],
                        pb[dbk]))
            act_rel = P.cnt["mm"]
            if is_moe:
                cbc_rel[cpar] = P.cnt["ve"]
        t_out = P.cnt["ve"]
        P.wait("sync", "ve", t_out)
        dst3 = dst.rearrange("(dc p) n -> p dc n", p=128)
        P.inc("sync", "xst", lambda e: e.dma_start(
            out=dst3[:, :, col0:col0 + HALF], in_=A3v), 16)

    def attn():
        x13 = x1.rearrange("(dc p) n -> p dc n", p=128)
        hc3 = hc_d.rearrange("(dc p) n -> p dc n", p=128)
        xa_buf = [A[:, 0:8192], A[:, 8192:16384]]
        h2_buf = [B1[:, 0:8192], B1[:, 8192:16384]]
        xa_rel = [0, 0]
        h2_rel = [0, 0]
        for tt in range(4):
            par = tt % 2
            xa3 = xa_buf[par].rearrange("p (dc n) -> p dc n", n=512)
            P.wait("sync", "ve", xa_rel[par])
            t_x = P.inc("sync", f"xt{par}", lambda e, tt=tt, xa3=xa3: e.dma_start(
                out=xa3, in_=x13[:, :, tt * 512:(tt + 1) * 512]), 16)
            P.wait("scalar", f"xt{par}", t_x)
            P.wait("tensor", f"xt{par}", t_x)
            P.wait("vector", f"xt{par}", t_x)
            P.wait("vector", f"ht{par}", h2_rel[par])
            ln_tile(lambda dc, par=par: xa_buf[par][:, dc * 512:(dc + 1) * 512],
                    lambda dc, par=par: h2_buf[par][:, dc * 512:(dc + 1) * 512],
                    router=False)
            xa_rel[par] = P.cnt["ve"]
            P.wait("sync", "ve", P.cnt["ve"])
            h23 = h2_buf[par].rearrange("p (dc n) -> p dc n", n=512)
            h2_rel[par] = P.inc("sync", f"ht{par}", lambda e, tt=tt, h23=h23: e.dma_start(
                out=hc3[:, :, tt * 512:(tt + 1) * 512], in_=h23), 16)
        P.wait("gpsimd", "ht0", P.cnt["ht0"])
        P.wait("gpsimd", "ht1", P.cnt["ht1"])
        for i in range(4):
            P.inc("gpsimd", "wq", lambda e, i=i: e.dma_start(out=wqkv[i], in_=wq_my[i]), 16)
        P.inc("gpsimd", "wq", lambda e: e.dma_start(out=wqkv[4], in_=wk_my), 16)
        t_wqkv = P.inc("gpsimd", "wq", lambda e: e.dma_start(out=wqkv[5], in_=wv_my), 16)
        P.wait("gpsimd", "ht0", P.cnt["ht0"])
        P.wait("gpsimd", "ht1", P.cnt["ht1"])
        t_cc1 = P.inc("gpsimd", "cc", lambda e: e.collective_compute(
            "AllGather", ALU.bypass,
            replica_groups=[[0, 1, 2, 3], [4, 5, 6, 7]],
            ins=[hc_d.opt()], outs=[hg_d.opt()]))
        P.barrier()

        hg3 = hg_d.rearrange("(blk dc p) n -> p blk dc n", p=128, dc=16)
        hgb_rel = [0, 0]
        vsb_rel = 0
        tpos_rel = 0
        for tt2 in range(16):
            par = tt2 % 2
            blk = tt2 // 4
            coff = (tt2 % 4) * 512
            P.wait("sync", "mm", hgb_rel[par])
            P.wait("sync", "cc", t_cc1)
            hb3 = hg_buf[par].rearrange("p (dc n) -> p dc n", n=512)
            t_hg = P.inc("sync", f"hg{par}", lambda e, hb3=hb3, blk=blk, coff=coff: e.dma_start(
                out=hb3, in_=hg3[:, blk, :, coff:coff + 512]), 16)
            P.wait("gpsimd", "mm", tpos_rel)
            P.inc("gpsimd", "gp", lambda e, tt2=tt2: e.iota(
                tpos, pattern=[[1, 512]], base=tt2 * 512, channel_multiplier=0,
                allow_small_or_imprecise_dtypes=True))
            t_ip = P.cnt["gp"]
            tpos_rel = mmgrp(6, [lambda e, st, sp: e.matmul(
                pb[6][0:64, :], invfq, tpos, start=st, stop=sp)],
                extra_waits=[("gp", t_ip)])
            bread(6, "vector", "ve", lambda e: e.tensor_scalar(
                kf_scr, pb[6][0:64, :], 1.0 / TWO_PI, None, ALU.mult),
                extra_waits=[("ve", P.cnt["ve"])])
            vstep(lambda e: e.tensor_copy(ki_scr, kf_scr))
            vstep(lambda e: e.tensor_copy(kf_scr, ki_scr))
            P.wait("vector", "mm", bmm[6])
            t_r1 = vstep(lambda e: e.scalar_tensor_tensor(
                r_scr, kf_scr, -C1, pb[6][0:64, :], ALU.mult, ALU.add))
            brel[6] = ("ve", t_r1)
            vstep(lambda e: e.scalar_tensor_tensor(
                r_scr, kf_scr, -C2, r_scr, ALU.mult, ALU.add))
            vstep(lambda e: e.scalar_tensor_tensor(
                r_scr, kf_scr, -C3, r_scr, ALU.mult, ALU.add))
            vstep(lambda e: e.tensor_scalar(
                kf_scr, r_scr, math.pi, -TWO_PI, ALU.is_gt, ALU.mult))
            vstep(lambda e: e.tensor_add(r_scr, r_scr, kf_scr))
            vstep(lambda e: e.tensor_scalar(rc_scr, r_scr, math.pi / 2, None, ALU.add))
            vstep(lambda e: e.tensor_scalar(
                kf_scr, rc_scr, math.pi, -TWO_PI, ALU.is_gt, ALU.mult))
            t_rr = vstep(lambda e: e.tensor_add(rc_scr, rc_scr, kf_scr))
            P.wait("scalar", "ve", t_rr)
            P.inc("scalar", "ac", lambda e: e.activation(sin_t[0:64, :], r_scr, AF.Sin, scale=-1.0))
            P.inc("scalar", "ac", lambda e: e.activation(sin_t[64:128, :], r_scr, AF.Sin))
            P.inc("scalar", "ac", lambda e: e.activation(cos_t[0:64, :], rc_scr, AF.Sin))
            t_cs = P.inc("scalar", "ac", lambda e: e.activation(cos_t[64:128, :], rc_scr, AF.Sin))
            P.wait("vector", "ac", t_cs)
            for j in range(6):
                bkq = j % 2
                t_qk = mmgrp(bkq, [
                    (lambda e, st, sp, dc=dc, j=j, par=par, bkq=bkq: e.matmul(
                        pb[bkq], wqkv[j][:, dc * 128:(dc + 1) * 128],
                        hg_buf[par][:, dc * 512:(dc + 1) * 512], start=st, stop=sp))
                    for dc in range(NDC)],
                    extra_waits=[("wq", t_wqkv), (f"hg{par}", t_hg)])
                if j < 5:
                    if j < 4:
                        dest = arQ[:, j * 8192 + tt2 * 512: j * 8192 + tt2 * 512 + 512]
                    else:
                        dest = K_sb[:, tt2 * 512: tt2 * 512 + 512]
                    P.wait("vector", "mm", t_qk)
                    P.wait("vector", "ve", P.cnt["ve"])
                    P.inc("vector", "ve", lambda e, bkq=bkq: e.tensor_mul(rope_a, pb[bkq], cos_t))
                    P.inc("vector", "ve", lambda e, bkq=bkq: e.tensor_mul(
                        rope_b[0:64, :], pb[bkq][64:128, :], sin_t[0:64, :]))
                    P.inc("vector", "ve", lambda e, bkq=bkq: e.tensor_mul(
                        rope_b[64:128, :], pb[bkq][0:64, :], sin_t[64:128, :]))
                    P.wait("vector", "ve", P.cnt["ve"])
                    t_rope = P.inc("vector", "ve", lambda e, dest=dest: e.tensor_add(
                        dest, rope_a, rope_b))
                    brel[bkq] = ("ve", t_rope)
                else:
                    t_vf = bread(bkq, "scalar", "ac", lambda e, bkq=bkq: e.activation(
                        vf_scr, pb[bkq], AF.Copy))
                    P.wait("tensor", "ac", t_vf)
                    P.wait("tensor", "ve", vsb_rel)
                    for ks in range(4):
                        P.inc("tensor", "mm", lambda e, ks=ks: e.transpose(
                            tpb[:, ks * 128:(ks + 1) * 128],
                            vf_scr[:, ks * 128:(ks + 1) * 128], identb))
                    t_tp = P.cnt["mm"]
                    P.wait("vector", "mm", t_tp)
                    vsb_rel = P.inc("vector", "ve", lambda e, tt2=tt2: e.tensor_copy(
                        V_sb[:, tt2 * 512:(tt2 + 1) * 512], tpb[:, 0:512]))
            hgb_rel[par] = P.cnt["mm"]
        P.barrier()

        obk = [5, 6]
        lbk = [2, 3]
        pp_rel = [0, 0, 0, 0]
        obc_rel = [0, 0]
        on_rel = 0
        sweep = 0
        oc3 = oc_d.rearrange("(hq p) n -> p hq n", p=128)
        for hq in range(NQH):
            for qg in range(16):
                sw = sweep % 2
                ob = obk[sw]
                lb = lbk[sw]
                if ob in brel:
                    P.wait("tensor", *brel.pop(ob))
                if lb in brel:
                    P.wait("tensor", *brel.pop(lb))
                nkb = qg + 1
                for kb in range(nkb):
                    for kc in range(4):
                        sb = kc % 2
                        if sb in brel:
                            P.wait("tensor", *brel.pop(sb))
                        t_s = P.inc("tensor", "mm", lambda e, sb=sb, kb=kb, kc=kc, hq=hq, qg=qg: e.matmul(
                            pb[sb], K_sb[:, kb * 512 + kc * 128: kb * 512 + kc * 128 + 128],
                            arQ[:, hq * 8192 + qg * 512: hq * 8192 + qg * 512 + 512],
                            start=True, stop=True))
                        bmm[sb] = t_s
                        t_e = bread(sb, "scalar", "ac", lambda e, sb=sb, kc=kc: e.activation(
                            pprime[kc], pb[sb], AF.Exp, bias=ebias, scale=RS),
                            extra_waits=[("mm", pp_rel[kc])])
                        gate_sem = ("ac", t_e)
                        if kb == qg:
                            P.wait("gpsimd", "ac", t_e)
                            t_gp = P.inc("gpsimd", "gp", lambda e, kc=kc: e.affine_select(
                                out=pprime[kc], in_=pprime[kc], compare_op=ALU.is_ge,
                                fill=0.0, base=-kc * 128, pattern=[[1, 512]],
                                channel_multiplier=-1))
                            gate_sem = ("gp", t_gp)
                        P.wait("tensor", gate_sem[0], gate_sem[1])
                        P.emit("tensor", lambda e, kb=kb, kc=kc, ob=ob, nkb=nkb: e.matmul(
                            pb[ob], V_sb[:, kb * 512 + kc * 128: kb * 512 + kc * 128 + 128],
                            pprime[kc], start=(kb == 0 and kc == 0),
                            stop=(kb == nkb - 1 and kc == 3)))
                        pp_rel[kc] = P.inc("tensor", "mm", lambda e, kc=kc, lb=lb, kb=kb, nkb=nkb: e.matmul(
                            pb[lb][0:1, :], onesb, pprime[kc],
                            start=(kb == 0 and kc == 0), stop=(kb == nkb - 1 and kc == 3)))
                bmm[ob] = bmm[lb] = P.cnt["mm"]
                ir = invl_row[sw]
                t_il = bread(lb, "vector", "ve", lambda e, ir=ir, lb=lb: e.reciprocal(
                    ir, pb[lb][0:1, :]),
                    extra_waits=[("ve", P.cnt["ve"]), ("mm", P.cnt["mm"])])
                P.wait("vector", "ve", t_il)
                t_il2 = P.inc("vector", "ve", lambda e, ir=ir, hq=hq: e.tensor_scalar(
                    ir, ir, hs_sb[0:1, hq:hq + 1], None, ALU.mult))
                mmgrp(4, [lambda e, st, sp, ir=ir: e.matmul(
                    pb[4], onesr, ir, start=st, stop=sp)],
                    extra_waits=[("ve", t_il2)])
                t_bc = bread(4, "vector", "ve", lambda e, sw=sw: e.tensor_copy(
                    obc_sb[sw], pb[4]), extra_waits=[("ve", obc_rel[sw])])
                t_on = bread(ob, "vector", "ve", lambda e, qg=qg, ob=ob, sw=sw: e.tensor_mul(
                    On_sb[:, qg * 512:(qg + 1) * 512], pb[ob], obc_sb[sw]),
                    extra_waits=[("oc", on_rel), ("ve", t_bc)])
                obc_rel[sw] = t_on
                sweep += 1
            P.wait("sync", "ve", P.cnt["ve"])
            on_rel = P.inc("sync", "oc", lambda e, hq=hq: e.dma_start(
                out=oc3[:, hq, :], in_=On_sb), 16)
        P.wait("gpsimd", "oc", P.cnt["oc"])
        P.inc("gpsimd", "cc", lambda e: e.collective_compute(
            "AllGather", ALU.bypass,
            replica_groups=[[0, 1, 2, 3], [4, 5, 6, 7]],
            ins=[oc_d.opt()], outs=[og_d.opt()]))
        P.barrier()

        offs = []

        def setup_regs(e):
            v = e.value_load(qsel_sb[0:1, 0:1])
            r = e.alloc_register("rcb")
            e.reg_alu(r, v, TOK, ALU.mult)
            base = e.snap(r)
            for tt in range(4):
                r2 = e.alloc_register(f"rcb{tt}")
                e.reg_alu(r2, base, tt * 512, ALU.add)
                offs.append(e.snap(r2))
        P.emit("gpsimd", setup_regs)
        arWO = arQ
        for oc in range(NDC):
            P.inc("gpsimd", "wo", lambda e, oc=oc: e.dma_start(
                out=arWO[:, oc * 2048:(oc + 1) * 2048], in_=wo_r[oc]), 16)
        t_wo = P.cnt["wo"]
        ogb_rel = [0, 0]
        x1t_rel = [0, 0]
        xo_rel = [0, 0]
        ogv = og_d.rearrange("(r hq p) n -> p r hq n", p=128, hq=4)
        for tt in range(4):
            par = tt % 2
            P.wait("gpsimd", "mm", ogb_rel[par])
            ob3 = og_buf[par].rearrange("p (hc n) -> p hc n", n=512)
            t_og = P.inc("gpsimd", f"og{par}", lambda e, tt=tt, ob3=ob3: e.dma_start(
                out=ob3.rearrange("p hc n -> p hc n") if False else ob3,
                in_=ogv[:, :, :, bass.ds(offs[tt], 512)].rearrange(
                    "p r hq n -> p (r hq) n")), 16)
            for oc in range(NDC):
                bk = oc % 2
                mmgrp(bk, [
                    (lambda e, st, sp, hc=hc, oc=oc, par=par, bk=bk: e.matmul(
                        pb[bk], arWO[:, oc * 2048 + hc * 128: oc * 2048 + (hc + 1) * 128],
                        og_buf[par][:, hc * 512:(hc + 1) * 512], start=st, stop=sp))
                    for hc in range(NDC)],
                    extra_waits=[(f"og{par}", t_og), ("wo", t_wo)])
                xp = oc % 2
                P.wait("sync", "ve", x1t_rel[xp])
                t_x1 = P.inc("sync", f"x6{xp}", lambda e, oc=oc, tt=tt, xp=xp: e.dma_start(
                    out=x1_t6[xp], in_=x1[oc * 128:(oc + 1) * 128, tt * 512:(tt + 1) * 512]), 16)
                t_add = bread(bk, "vector", "ve", lambda e, xp=xp, bk=bk: e.tensor_add(
                    xo_t6[xp], x1_t6[xp], pb[bk]),
                    extra_waits=[(f"x6{xp}", t_x1), (f"xo{xp}", xo_rel[xp])])
                x1t_rel[xp] = t_add
                P.wait("sync", "ve", t_add)
                xo_rel[xp] = P.inc("sync", f"xo{xp}", lambda e, oc=oc, tt=tt, xp=xp: e.dma_start(
                    out=x2[oc * 128:(oc + 1) * 128, tt * 512:(tt + 1) * 512], in_=xo_t6[xp]), 16)
            ogb_rel[par] = P.cnt["mm"]
        P.barrier()

    if "ffn1" in phases:
        for half in range(2):
            ffn_half(x_in, x1, half)
            P.barrier()
    if "attn" in phases:
        attn()
    if "ffn2" in phases:
        for half in range(2):
            ffn_half(x2, y, half)
            P.barrier()
    P.wait("sync", "xst", P.cnt["xst"])

    P.run()
    stack.close()
    return nc


def _prep_shared(i):
    wg = np.ascontiguousarray(
        i["L0_ffn_w_gate"].reshape(16, 16, 128, 11, 128).transpose(0, 3, 2, 1, 4))
    wu = np.ascontiguousarray(
        i["L0_ffn_w_up"].reshape(16, 16, 128, 11, 128).transpose(0, 3, 2, 1, 4))
    wgu = np.concatenate([wg, wu], axis=4)
    wgu_r = np.ascontiguousarray(wgu.reshape(16, 11, 128, 4096))
    wd_r = np.ascontiguousarray(
        i["L0_ffn_w_down"].reshape(16, 11, 128, 16, 128).transpose(0, 3, 2, 1, 4)
        .reshape(16, 16, 128, 1408))
    shg = i["L0_ffn_sh_gate"].reshape(16, 128, 44, 128).transpose(2, 1, 0, 3)
    shu = i["L0_ffn_sh_up"].reshape(16, 128, 44, 128).transpose(2, 1, 0, 3)
    shgu = np.concatenate([shg, shu], axis=3)
    shgu_r = np.ascontiguousarray(shgu.reshape(44, 128, 4096))
    shd_r = np.ascontiguousarray(
        i["L0_ffn_sh_down"].reshape(44, 128, 16, 128).transpose(2, 1, 0, 3)
        .reshape(16, 128, 5632))
    rt_r = np.ascontiguousarray(
        i["L0_ffn_router_w"].T.reshape(16, 128, 16).transpose(1, 0, 2).reshape(128, 256))
    sgw_r = np.ascontiguousarray(i["L0_ffn_sh_gate_w"].reshape(16, 128).T)
    wo_r = np.ascontiguousarray(
        i["L0_attn_Wo"].reshape(16, 128, 16, 128).transpose(2, 1, 0, 3).reshape(16, 128, 2048))
    return dict(wgu_r=wgu_r, wd_r=wd_r, shgu_r=shgu_r, shd_r=shd_r,
                rt_r=rt_r, sgw_r=sgw_r, wo_r=wo_r)


def _prep_core(i, c, shared):
    s, g = c // 4, c % 4
    x = i["L0_attn_x"]
    x_fm = np.ascontiguousarray(x[s, g * 2048:(g + 1) * 2048, :].T)
    wq = i["L0_attn_Wq"][:, g * 512:(g + 1) * 512]
    wq_my = np.ascontiguousarray(
        wq.reshape(16, 128, 4, 128).transpose(2, 1, 0, 3).reshape(4, 128, 2048))
    wk = i["L0_attn_Wk"][:, g * 128:(g + 1) * 128]
    wk_my = np.ascontiguousarray(
        wk.reshape(16, 128, 128).transpose(1, 0, 2).reshape(128, 2048))
    wv = i["L0_attn_Wv"][:, g * 128:(g + 1) * 128]
    wv_my = np.ascontiguousarray(
        wv.reshape(16, 128, 128).transpose(1, 0, 2).reshape(128, 2048))
    hs_my = np.ascontiguousarray(
        i["L0_attn_head_scale"][0, g * 4:(g + 1) * 4, 0, 0].reshape(1, 4))
    m = dict(shared)
    m.update(x_fm=x_fm, wq_my=wq_my, wk_my=wk_my, wv_my=wv_my, hs_my=hs_my,
             qsel=np.array([[g]], dtype=np.int32))
    return m


def kernel(**inputs):
    shared = _prep_shared(inputs)
    in_maps = [_prep_core(inputs, c, shared) for c in range(8)]
    nc = build_program()
    res = run_bass_kernel_spmd(nc, in_maps, core_ids=list(range(8)))
    out = np.empty((2, 8192, 2048), dtype=np.float32)
    for c in range(8):
        s, g = c // 4, c % 4
        out[s, g * 2048:(g + 1) * 2048, :] = res.results[c]["y"].T
    return out


# version 03
# speedup vs baseline: 1.1428x; 1.1428x over previous
import math
import numpy as np
import ml_dtypes
from contextlib import ExitStack

import concourse.bass as bass
import concourse.mybir as mybir
from concourse.bass_utils import run_bass_kernel_spmd

F32 = mybir.dt.float32
BF16 = mybir.dt.bfloat16
I32 = mybir.dt.int32
AF = mybir.ActivationFunctionType
ALU = mybir.AluOpType
AX = mybir.AxisListType

D = 2048
NDC = 16
TOK = 2048
HALF = 1024
TT = 512
IMOE = 1408
NIC = 11
NEXP = 16
SEQ = 8192
NQH = 4
RS = 1.0 / math.sqrt(128.0)
MOFF = 16.0
EPS = 1e-5
TWO_PI = 2.0 * math.pi
C1 = float(round(TWO_PI * 2 ** 10) / 2 ** 10)
C2 = float(np.float32(TWO_PI - C1))
C3 = float(np.float32(TWO_PI - C1 - np.float64(np.float32(TWO_PI - C1))))
LN1E4 = math.log(10000.0)

ENGINES = ("sync", "tensor", "vector", "scalar", "gpsimd")
ALLSEMS = ("cc", "mm", "ve", "ac", "gp",
           "xi", "xg", "xa", "xc", "xst", "wq", "wo", "oc",
           "cr0", "cr1", "wg0", "wg1", "wd0", "wd1",
           "xt0", "xt1", "ht0", "ht1", "hg0", "hg1",
           "og0", "og1", "x60", "x61", "xo0", "xo1")


class Prog:
    def __init__(self, nc, stack):
        self.nc = nc
        self.items = {e: [] for e in ENGINES}
        self.sems = {}
        self.cnt = {}
        self.stack = stack
        self.nbar = 0

    def sem(self, name):
        self.sems[name] = self.stack.enter_context(self.nc.semaphore(name))
        self.cnt[name] = 0

    def emit(self, eng, fn):
        self.items[eng].append(fn)

    def inc(self, eng, sem, fn, n=1):
        self.cnt[sem] += n
        tgt = self.cnt[sem]
        s = self.sems[sem]

        def run(e, fn=fn, s=s, n=n):
            fn(e).then_inc(s, n)
        self.items[eng].append(run)
        return tgt

    def wait(self, eng, sem, val):
        if val <= 0:
            return
        assert val <= self.cnt[sem], f"wait {sem}>={val} planned={self.cnt[sem]}"
        s = self.sems[sem]
        self.items[eng].append(lambda e, s=s, v=val: e.wait_ge(s, v))

    def barrier(self):
        waits = [(s, self.cnt[s]) for s in ALLSEMS]
        self.nbar += 1
        tgt = 5 * self.nbar
        bs = self.sems["bar"]
        for eng in ENGINES:
            for s, v in waits:
                self.wait(eng, s, v)

            def run(e, bs=bs, tgt=tgt):
                e.nop().then_inc(bs, 1)
                e.wait_ge(bs, tgt)
            self.items[eng].append(run)
        self.cnt["bar"] = tgt

    def run(self):
        with self.nc.Block() as blk:
            for eng, starter in (("gpsimd", blk.gpsimd), ("scalar", blk.scalar),
                                 ("tensor", blk.tensor), ("vector", blk.vector),
                                 ("sync", blk.sync)):
                items = self.items[eng]

                def body(e, items=items):
                    for fn in items:
                        fn(e)
                starter(body)


def build_program(debug_taps=(), phases=("ffn1", "attn", "ffn2")):
    nc = bass.Bass("TRN2", target_bir_lowering=False, debug=False)
    stack = ExitStack()

    def kind_of(name):
        return "ExternalOutput" if name in debug_taps else "Internal"

    x_in = nc.dram_tensor("x_fm", [D, TOK], F32, kind="ExternalInput").ap()
    wgu_r = nc.dram_tensor("wgu_r", [NEXP, NIC, 128, 4096], F32, kind="ExternalInput").ap()
    wd_r = nc.dram_tensor("wd_r", [NEXP, NDC, 128, IMOE], F32, kind="ExternalInput").ap()
    shgu_r = nc.dram_tensor("shgu_r", [44, 128, 4096], F32, kind="ExternalInput").ap()
    shd_r = nc.dram_tensor("shd_r", [NDC, 128, 5632], F32, kind="ExternalInput").ap()
    rt_r = nc.dram_tensor("rt_r", [128, 256], F32, kind="ExternalInput").ap()
    sgw_r = nc.dram_tensor("sgw_r", [128, 16], F32, kind="ExternalInput").ap()
    wq_my = nc.dram_tensor("wq_my", [NQH, 128, D], F32, kind="ExternalInput").ap()
    wk_my = nc.dram_tensor("wk_my", [128, D], F32, kind="ExternalInput").ap()
    wv_my = nc.dram_tensor("wv_my", [128, D], F32, kind="ExternalInput").ap()
    wo_r = nc.dram_tensor("wo_r", [NDC, 128, D], F32, kind="ExternalInput").ap()
    hs_my = nc.dram_tensor("hs_my", [1, NQH], F32, kind="ExternalInput").ap()
    qsel = nc.dram_tensor("qsel", [1, 1], I32, kind="ExternalInput").ap()

    x1 = nc.dram_tensor("x1", [D, TOK], F32, kind=kind_of("x1")).ap()
    x2 = nc.dram_tensor("x2", [D, TOK], F32, kind=kind_of("x2")).ap()
    y = nc.dram_tensor("y", [D, TOK], F32, kind="ExternalOutput").ap()
    comb_d = nc.dram_tensor("comb_d", [NEXP, HALF], F32, kind=kind_of("comb_d")).ap()
    hc_d = nc.dram_tensor("hc_d", [D, TOK], BF16, kind="Internal").ap()
    hg_d = nc.dram_tensor("hg_d", [4 * D, TOK], BF16, kind=kind_of("hg_d")).ap()
    oc_d = nc.dram_tensor("oc_d", [NQH * 128, SEQ], BF16, kind="Internal").ap()
    og_d = nc.dram_tensor("og_d", [4 * NQH * 128, SEQ], BF16, kind=kind_of("og_d")).ap()
    lrow_d = nc.dram_tensor("lrow_d", [2, TT], F32, kind="Internal").ap()

    A_t = nc.alloc_sbuf_tensor("A", [128, 16384], F32)
    A = A_t.ap()
    arQ = A_t.bitcast(BF16).ap()
    B1 = nc.alloc_sbuf_tensor("B1", [128, 16384], BF16).ap()
    B2 = nc.alloc_sbuf_tensor("B2", [128, 16384], BF16).ap()
    B3 = nc.alloc_sbuf_tensor("B3", [128, 8192], BF16).ap()
    B4 = nc.alloc_sbuf_tensor("B4", [128, 8192], BF16).ap()
    F2_t = nc.alloc_sbuf_tensor("F2", [128, 6144], F32)
    F2 = F2_t.ap()
    F2b = F2_t.bitcast(BF16).ap()
    RW_t = nc.alloc_sbuf_tensor("RW", [128, 2688], F32)
    RW = RW_t.ap()
    RWi = RW_t.bitcast(I32).ap()
    rt_sb = nc.alloc_sbuf_tensor("rt_sb", [128, 256], F32).ap()
    sgw_sb = nc.alloc_sbuf_tensor("sgw_sb", [128, 16], BF16).ap()
    hs_sb = nc.alloc_sbuf_tensor("hs_sb", [1, NQH], F32).ap()
    qsel_sb = nc.alloc_sbuf_tensor("qsel_sb", [1, 1], I32).ap()
    onesc = nc.alloc_sbuf_tensor("onesc", [128, 1], F32).ap()
    onesb = nc.alloc_sbuf_tensor("onesb", [128, 1], BF16).ap()
    onesr = nc.alloc_sbuf_tensor("onesr", [1, 128], F32).ap()
    identb = nc.alloc_sbuf_tensor("identb", [128, 128], BF16).ap()
    identf = nc.alloc_sbuf_tensor("identf", [128, 128], F32).ap()
    ebias = nc.alloc_sbuf_tensor("ebias", [128, 1], F32).ap()
    epsc = nc.alloc_sbuf_tensor("epsc", [1, 1], F32).ap()
    invfq = nc.alloc_sbuf_tensor("invfq", [1, 64], F32).ap()

    h_sb = B1
    act_sb = B2
    wgu_buf = [B3[:, 0:4096], B3[:, 4096:8192]]
    wd_buf = [B4[:, 0:IMOE], B4[:, IMOE:2 * IMOE]]
    silu_scr = [B4[:, 2816 + i * 512: 2816 + (i + 1) * 512] for i in range(2)]
    au_scr = [B4[:, 3840 + i * 512: 3840 + (i + 1) * 512] for i in range(2)]
    comb_bc = [F2[:, 0:1024], F2[:, 1024:2048]]
    sg_bc = F2[:, 2048:3072]
    t1_scr = [F2[:, 3072 + i * 512: 3072 + (i + 1) * 512] for i in range(2)]
    sq_scr = [F2[:, 4096 + i * 512: 4096 + (i + 1) * 512] for i in range(2)]
    K_sb = B2[:, 0:8192]
    hg_buf = [B2[:, 8192:16384], B4[:, 0:8192]]
    V_sb = B3[:, 0:8192]
    On_sb = B1[:, 0:8192]
    pprime = [B1[:, 8192 + i * 512: 8192 + (i + 1) * 512] for i in range(4)]
    vf_scr = B1[:, 12288:12800]
    og_buf = [B1[:, 0:8192], B1[:, 8192:16384]]
    cos_t = F2[:, 0:512]
    sin_t = F2[:, 512:1024]
    rope_a = F2[:, 1024:1536]
    rope_b = F2[:, 1536:2048]
    obc_sb = [F2[:, 2048 + i * 512: 2048 + (i + 1) * 512] for i in range(2)]
    x1_t6 = [F2[:, 3072 + i * 512: 3072 + (i + 1) * 512] for i in range(2)]
    xo_t6 = [F2[:, 4096 + i * 512: 4096 + (i + 1) * 512] for i in range(2)]
    wqkv = [B1[:, i * 2048:(i + 1) * 2048] for i in range(6)]

    m_row = RW[0:1, 0:512]
    v_row = RW[0:1, 512:1024]
    rstd_row = RW[0:1, 1024:1536]
    msq_row = RW[0:1, 1536:2048]
    comb_stg = RW[0:16, 2048:2560]
    lg_s = RW[:, 2560:2576]
    vwork = RW[:, 2576:2592]
    elx = RW[:, 2592:2608]
    selx = RW[:, 2608:2624]
    nmax_c = RW[:, 2624:2625]
    t4n_c = RW[:, 2625:2626]
    s4_c = RW[:, 2626:2627]
    inv_c = RW[:, 2627:2628]
    rstd_cols = RW[:, 2628:2632]
    theta_unused = None
    kf_scr = RW[0:64, 512:1024]
    ki_scr = RWi[0:64, 512:1024]
    r_scr = RW[0:64, 1024:1536]
    rc_scr = RW[0:64, 1536:2048]
    tpos = RW[0:1, 2048:2560]
    invl_row = [RW[0:1, 0:512], RW[0:1, 512:1024]]
    crow = [RW[0:1, 0:1024], RW[0:1, 1024:2048]]

    pb = [nc.alloc_psum_tensor(f"pb{i}", [128, 512], F32).ap() for i in range(7)]
    tpb = nc.alloc_psum_tensor("tpb", [128, 1024], BF16).ap()

    P = Prog(nc, stack)
    P.sem("bar")
    for s in ALLSEMS:
        P.sem(s)

    brel = {}
    bmm = {}

    def mmgrp(bk, emits, extra_waits=()):
        if bk in brel:
            P.wait("tensor", *brel.pop(bk))
        for s, v in extra_waits:
            P.wait("tensor", s, v)
        n = len(emits)
        t = None
        for i, fn in enumerate(emits):
            st, sp = (i == 0), (i == n - 1)
            if sp:
                t = P.inc("tensor", "mm", lambda e, fn=fn, st=st, sp=sp: fn(e, st, sp))
            else:
                P.emit("tensor", lambda e, fn=fn, st=st, sp=sp: fn(e, st, sp))
        bmm[bk] = t
        return t

    def bread(bk, eng, sem, fn, extra_waits=()):
        P.wait(eng, "mm", bmm[bk])
        for s, v in extra_waits:
            P.wait(eng, s, v)
        t = P.inc(eng, sem, fn)
        brel[bk] = (sem, t)
        return t

    def vstep(fn, wait_self=True):
        if wait_self:
            P.wait("vector", "ve", P.cnt["ve"])
        return P.inc("vector", "ve", fn)

    P.inc("sync", "xi", lambda e: e.dma_start(out=rt_sb, in_=rt_r), 16)
    P.inc("sync", "xi", lambda e: e.dma_start(out=hs_sb, in_=hs_my), 16)
    P.inc("sync", "xi", lambda e: e.dma_start(out=qsel_sb, in_=qsel), 16)
    P.inc("gpsimd", "xg", lambda e: e.dma_start(out=sgw_sb, in_=sgw_r), 16)
    for ap_, val in ((onesc, 1.0), (onesb, 1.0), (onesr, 1.0), (ebias, -MOFF), (epsc, EPS)):
        P.emit("vector", lambda e, ap_=ap_, val=val: e.memset(ap_, val))
    P.inc("vector", "ve", lambda e: e.memset(RW[:, 0:2688], 0.0))
    t_ms = P.inc("gpsimd", "gp", lambda e: e.memset(identb, 0.0))
    P.wait("gpsimd", "gp", t_ms)
    P.inc("gpsimd", "gp", lambda e: e.affine_select(
        out=identb, in_=identb, compare_op=ALU.not_equal, fill=1.0,
        base=0, pattern=[[-1, 128]], channel_multiplier=1))
    t_ms = P.inc("gpsimd", "gp", lambda e: e.memset(identf, 0.0))
    P.wait("gpsimd", "gp", t_ms)
    P.inc("gpsimd", "gp", lambda e: e.affine_select(
        out=identf, in_=identf, compare_op=ALU.not_equal, fill=1.0,
        base=0, pattern=[[-1, 128]], channel_multiplier=1))
    P.inc("gpsimd", "gp", lambda e: e.iota(
        invfq, pattern=[[1, 64]], base=0, channel_multiplier=0,
        allow_small_or_imprecise_dtypes=True))
    P.wait("scalar", "gp", P.cnt["gp"])
    P.inc("scalar", "ac", lambda e: e.activation(invfq, invfq, AF.Exp, scale=-LN1E4 / 64.0))
    P.barrier()

    def ln_tile(adc, h_out, router=False, comb_col0=0):
        sq_rel = [0, 0]
        for dc in range(NDC):
            par = dc % 2
            P.wait("scalar", "mm", sq_rel[par])
            t_ac = P.inc("scalar", "ac", lambda e, dc=dc, par=par: e.activation(
                sq_scr[par], adc(dc), AF.Square))
            if dc == 0:
                for bk in (0, 1):
                    if bk in brel:
                        P.wait("tensor", *brel.pop(bk))
            P.emit("tensor", lambda e, dc=dc: e.matmul(
                pb[0][0:1, :], onesc, adc(dc), start=(dc == 0), stop=(dc == NDC - 1)))
            P.wait("tensor", "ac", t_ac)
            sq_rel[par] = P.inc("tensor", "mm", lambda e, dc=dc, par=par: e.matmul(
                pb[1][0:1, :], onesc, sq_scr[par], start=(dc == 0), stop=(dc == NDC - 1)))
        bmm[0] = bmm[1] = P.cnt["mm"]
        bread(0, "vector", "ve", lambda e: e.tensor_scalar(
            m_row, pb[0][0:1, :], 1.0 / D, None, ALU.mult))
        bread(1, "vector", "ve", lambda e: e.tensor_scalar(
            v_row, pb[1][0:1, :], 1.0 / D, None, ALU.mult))
        vstep(lambda e: e.tensor_mul(msq_row, m_row, m_row))
        t_var = vstep(lambda e: e.tensor_sub(v_row, v_row, msq_row))
        P.wait("scalar", "ve", t_var)
        t_ln = P.inc("scalar", "ac", lambda e: e.activation(rstd_row, v_row, AF.Ln, bias=epsc))
        P.wait("scalar", "ac", t_ln)
        t_rstd = P.inc("scalar", "ac", lambda e: e.activation(
            rstd_row, rstd_row, AF.Exp, scale=-0.5))
        t_ve_rows = P.cnt["ve"]
        mmgrp(2, [lambda e, st, sp: e.matmul(pb[2], onesr, m_row, start=st, stop=sp)],
              extra_waits=[("ve", t_ve_rows)])
        mmgrp(3, [lambda e, st, sp: e.matmul(pb[3], onesr, rstd_row, start=st, stop=sp)],
              extra_waits=[("ac", t_rstd)])
        lgb = [4, 5, 6, 0]
        t1_rel = [0, 0]
        P.wait("vector", "mm", bmm[3])
        last_norm = 0
        for dc in range(NDC):
            par = dc % 2
            if router:
                P.wait("vector", "mm", t1_rel[par])
            P.inc("vector", "ve", lambda e, dc=dc, par=par: e.tensor_sub(
                t1_scr[par], adc(dc), pb[2]))
            P.wait("vector", "ve", P.cnt["ve"])
            t_n = P.inc("vector", "ve", lambda e, dc=dc, par=par: e.tensor_mul(
                h_out(dc), t1_scr[par], pb[3]))
            last_norm = t_n
            if router:
                P.wait("tensor", "ve", t_n)
                for sub in range(4):
                    bk = lgb[sub]
                    if dc == 0 and bk in brel:
                        P.wait("tensor", *brel.pop(bk))
                    fn = (lambda e, dc=dc, par=par, sub=sub, bk=bk: e.matmul(
                        pb[bk][0:128, 0:16], t1_scr[par][:, sub * 128:(sub + 1) * 128],
                        rt_sb[:, dc * 16:(dc + 1) * 16],
                        start=(dc == 0), stop=(dc == NDC - 1)))
                    if sub == 3:
                        t1_rel[par] = P.inc("tensor", "mm", fn)
                    else:
                        P.emit("tensor", fn)
        brel[2] = ("ve", last_norm)
        brel[3] = ("ve", last_norm)
        if not router:
            return last_norm
        for sub in range(4):
            bmm[lgb[sub]] = P.cnt["mm"]
        mmgrp(1, [
            (lambda e, st, sp, sub=sub: e.transpose(
                pb[1][:, sub:sub + 1], rstd_row[0:1, sub * 128:(sub + 1) * 128],
                onesr[0:1, 0:1]))
            for sub in range(4)], extra_waits=[("ac", t_rstd)])
        bread(1, "vector", "ve", lambda e: e.tensor_copy(rstd_cols, pb[1][:, 0:4]))
        for sub in range(4):
            bk = lgb[sub]
            t0 = bread(bk, "vector", "ve", lambda e, bk=bk, sub=sub: e.tensor_scalar(
                lg_s, pb[bk][0:128, 0:16], rstd_cols[:, sub:sub + 1], None, ALU.mult),
                extra_waits=[("ve", P.cnt["ve"]), ("mm", P.cnt["mm"])])
            vstep(lambda e: e.tensor_copy(vwork, lg_s))
            for j in range(3):
                vstep(lambda e: e.reduce_max(nmax_c, vwork, axis=AX.X, negate=True))
                vstep(lambda e: e.tensor_scalar(elx, vwork, nmax_c, None, ALU.add))
                vstep(lambda e: e.tensor_scalar(elx, elx, 0.0, -1e30, ALU.is_ge, ALU.mult))
                vstep(lambda e: e.tensor_add(vwork, vwork, elx))
            t_t4 = vstep(lambda e: e.reduce_max(t4n_c, vwork, axis=AX.X, negate=True))
            P.wait("scalar", "ve", t_t4)
            t_el = P.inc("scalar", "ac", lambda e: e.activation(elx, lg_s, AF.Exp, bias=t4n_c))
            vstep(lambda e: e.tensor_scalar(selx, lg_s, t4n_c, None, ALU.add))
            vstep(lambda e: e.tensor_scalar(selx, selx, 0.0, None, ALU.is_ge))
            P.wait("vector", "ac", t_el)
            vstep(lambda e: e.tensor_mul(elx, elx, selx))
            vstep(lambda e: e.reduce_sum(s4_c, elx, axis=AX.X))
            vstep(lambda e: e.reciprocal(inv_c, s4_c))
            t_cb = vstep(lambda e: e.tensor_scalar(lg_s, elx, inv_c, 0.5, ALU.mult, ALU.mult))
            mmgrp(1, [lambda e, st, sp, sub=sub: e.transpose(
                pb[1][0:16, sub * 128:(sub + 1) * 128], lg_s, identf)],
                extra_waits=[("ve", t_cb)])
        t_stg = bread(1, "vector", "ve", lambda e: e.tensor_copy(comb_stg, pb[1][0:16, 0:512]),
                      extra_waits=[("xc", P.cnt["xc"])])
        P.wait("sync", "ve", t_stg)
        t_cd = P.inc("sync", "xc", lambda e, comb_col0=comb_col0: e.dma_start(
            out=comb_d[:, comb_col0:comb_col0 + 512], in_=comb_stg), 16)
        return t_cd

    def ffn_half(src, dst, half):
        col0 = half * HALF
        src3 = src.rearrange("(dc p) n -> p dc n", p=128)
        A3v = A.rearrange("p (dc n) -> p dc n", n=1024)
        t_A = P.inc("sync", "xa", lambda e: e.dma_start(
            out=A3v, in_=src3[:, :, col0:col0 + HALF]), 16)
        P.wait("scalar", "xa", t_A)
        P.wait("tensor", "xa", t_A)
        P.wait("vector", "xa", t_A)

        def adc(dc, tt):
            return A[:, dc * 1024 + tt * 512: dc * 1024 + tt * 512 + 512]

        comb_done = 0
        for tt in range(2):
            comb_done = ln_tile(
                lambda dc, tt=tt: adc(dc, tt),
                lambda dc, tt=tt: h_sb[:, dc * 1024 + tt * 512: dc * 1024 + tt * 512 + 512],
                router=True, comb_col0=tt * 512)
        t_hdone = P.cnt["ve"]
        for tt in range(2):
            mmgrp(6, [
                (lambda e, st, sp, dc=dc, tt=tt: e.matmul(
                    pb[6][0:1, :], sgw_sb[:, dc:dc + 1],
                    h_sb[:, dc * 1024 + tt * 512: dc * 1024 + tt * 512 + 512],
                    start=st, stop=sp))
                for dc in range(NDC)],
                extra_waits=[("ve", t_hdone), ("xg", P.cnt["xg"])])
            t_sg = bread(6, "scalar", "ac", lambda e: e.activation(
                msq_row, pb[6][0:1, :], AF.Sigmoid))
            P.wait("scalar", "ac", t_sg)
            t_sg2 = P.inc("scalar", "ac", lambda e: e.mul(msq_row, msq_row, 0.5))
            mmgrp(6, [lambda e, st, sp: e.matmul(pb[6], onesr, msq_row, start=st, stop=sp)],
                  extra_waits=[("ac", t_sg2)])
            bread(6, "vector", "ve", lambda e, tt=tt: e.tensor_copy(
                sg_bc[:, tt * 512:(tt + 1) * 512], pb[6]))
        t_sgbc = P.cnt["ve"]

        wgu_rel = [0, 0]
        wd_rel = [0, 0]
        mul_rel = [0, 0]
        cbc_rel = [0, 0]
        crow_rel = [0, 0]
        act_rel = 0
        for xi in range(20):
            is_moe = xi < 16
            if is_moe:
                e_idx = xi
                cpar = xi % 2
                P.wait("gpsimd", "xc", comb_done)
                P.wait("gpsimd", "mm", crow_rel[cpar])
                t_cr = P.inc("gpsimd", f"cr{cpar}", lambda e, e_idx=e_idx, cpar=cpar: e.dma_start(
                    out=crow[cpar], in_=comb_d[e_idx:e_idx + 1, :]), 16)
                for tt in range(2):
                    mmgrp(6, [lambda e, st, sp, cpar=cpar, tt=tt: e.matmul(
                        pb[6], onesr, crow[cpar][0:1, tt * 512:(tt + 1) * 512],
                        start=st, stop=sp)],
                        extra_waits=[(f"cr{cpar}", t_cr), ("ve", cbc_rel[cpar])])
                    t_cp = bread(6, "vector", "ve", lambda e, cpar=cpar, tt=tt: e.tensor_copy(
                        comb_bc[cpar][:, tt * 512:(tt + 1) * 512], pb[6]))
                crow_rel[cpar] = P.cnt["mm"]
                gate = comb_bc[cpar]
                gate_ready = ("ve", t_cp)
            else:
                s_idx = xi - 16
                gate = sg_bc
                gate_ready = ("ve", t_sgbc)
            for ic in range(NIC):
                wpar = (xi * NIC + ic) % 2
                P.wait("gpsimd", "mm", wgu_rel[wpar])
                if is_moe:
                    t_w = P.inc("gpsimd", f"wg{wpar}", lambda e, e_idx=e_idx, ic=ic, wpar=wpar: e.dma_start(
                        out=wgu_buf[wpar], in_=wgu_r[e_idx, ic]), 16)
                else:
                    gic = s_idx * NIC + ic
                    t_w = P.inc("gpsimd", f"wg{wpar}", lambda e, gic=gic, wpar=wpar: e.dma_start(
                        out=wgu_buf[wpar], in_=shgu_r[gic]), 16)
                for tt in range(2):
                    gbk, ubk = tt, 2 + tt
                    ew = [(f"wg{wpar}", t_w)]
                    if xi == 0 and ic == 0:
                        ew.append(("ve", t_hdone))
                    mmgrp(gbk, [
                        (lambda e, st, sp, dc=dc, wpar=wpar, tt=tt, gbk=gbk: e.matmul(
                            pb[gbk], wgu_buf[wpar][:, dc * 256: dc * 256 + 128],
                            h_sb[:, dc * 1024 + tt * 512: dc * 1024 + tt * 512 + 512],
                            start=st, stop=sp))
                        for dc in range(NDC)], extra_waits=ew)
                    t_u = mmgrp(ubk, [
                        (lambda e, st, sp, dc=dc, wpar=wpar, tt=tt, ubk=ubk: e.matmul(
                            pb[ubk], wgu_buf[wpar][:, dc * 256 + 128: dc * 256 + 256],
                            h_sb[:, dc * 1024 + tt * 512: dc * 1024 + tt * 512 + 512],
                            start=st, stop=sp))
                        for dc in range(NDC)])
                    wgu_rel[wpar] = t_u
                    spar = (ic * 2 + tt) % 2
                    t_si = bread(gbk, "scalar", "ac",
                                 lambda e, spar=spar, gbk=gbk: e.activation(
                                     silu_scr[spar], pb[gbk], AF.Silu),
                                 extra_waits=[("ve", mul_rel[spar])])
                    t_au = bread(ubk, "vector", "ve",
                                 lambda e, spar=spar, ubk=ubk, gate=gate, tt=tt: e.tensor_mul(
                                     au_scr[spar], pb[ubk], gate[:, tt * 512:(tt + 1) * 512]),
                                 extra_waits=[gate_ready, ("ve", mul_rel[spar])])
                    P.wait("vector", "ac", t_si)
                    P.wait("vector", "mm", act_rel)
                    P.wait("vector", "ve", P.cnt["ve"])
                    mul_rel[spar] = P.inc("vector", "ve", lambda e, ic=ic, tt=tt, spar=spar: e.tensor_mul(
                        act_sb[:, ic * 1024 + tt * 512: ic * 1024 + tt * 512 + 512],
                        silu_scr[spar], au_scr[spar]))
            t_act = P.cnt["ve"]
            for oc in range(NDC):
                wpar = (xi * NDC + oc) % 2
                P.wait("gpsimd", "mm", wd_rel[wpar])
                if is_moe:
                    t_w = P.inc("gpsimd", f"wd{wpar}", lambda e, e_idx=e_idx, oc=oc, wpar=wpar: e.dma_start(
                        out=wd_buf[wpar], in_=wd_r[e_idx, oc]), 16)
                else:
                    t_w = P.inc("gpsimd", f"wd{wpar}", lambda e, s_idx=s_idx, oc=oc, wpar=wpar: e.dma_start(
                        out=wd_buf[wpar], in_=shd_r[oc][:, s_idx * IMOE:(s_idx + 1) * IMOE]), 16)
                for tt in range(2):
                    dbk = 4 + ((oc * 2 + tt) % 3)
                    t_d = mmgrp(dbk, [
                        (lambda e, st, sp, ic=ic, wpar=wpar, tt=tt, dbk=dbk: e.matmul(
                            pb[dbk], wd_buf[wpar][:, ic * 128:(ic + 1) * 128],
                            act_sb[:, ic * 1024 + tt * 512: ic * 1024 + tt * 512 + 512],
                            start=st, stop=sp))
                        for ic in range(NIC)],
                        extra_waits=[(f"wd{wpar}", t_w), ("ve", t_act)])
                    wd_rel[wpar] = t_d
                    bread(dbk, "vector", "ve", lambda e, oc=oc, tt=tt, dbk=dbk: e.tensor_add(
                        A[:, oc * 1024 + tt * 512: oc * 1024 + tt * 512 + 512],
                        A[:, oc * 1024 + tt * 512: oc * 1024 + tt * 512 + 512],
                        pb[dbk]))
            act_rel = P.cnt["mm"]
            if is_moe:
                cbc_rel[cpar] = P.cnt["ve"]
        t_out = P.cnt["ve"]
        P.wait("sync", "ve", t_out)
        dst3 = dst.rearrange("(dc p) n -> p dc n", p=128)
        P.inc("sync", "xst", lambda e: e.dma_start(
            out=dst3[:, :, col0:col0 + HALF], in_=A3v), 16)

    def attn():
        x13 = x1.rearrange("(dc p) n -> p dc n", p=128)
        hc3 = hc_d.rearrange("(dc p) n -> p dc n", p=128)
        xa_buf = [A[:, 0:8192], A[:, 8192:16384]]
        h2_buf = [B1[:, 0:8192], B1[:, 8192:16384]]
        xa_rel = [0, 0]
        h2_rel = [0, 0]
        for tt in range(4):
            par = tt % 2
            xa3 = xa_buf[par].rearrange("p (dc n) -> p dc n", n=512)
            P.wait("sync", "ve", xa_rel[par])
            t_x = P.inc("sync", f"xt{par}", lambda e, tt=tt, xa3=xa3: e.dma_start(
                out=xa3, in_=x13[:, :, tt * 512:(tt + 1) * 512]), 16)
            P.wait("scalar", f"xt{par}", t_x)
            P.wait("tensor", f"xt{par}", t_x)
            P.wait("vector", f"xt{par}", t_x)
            P.wait("vector", f"ht{par}", h2_rel[par])
            ln_tile(lambda dc, par=par: xa_buf[par][:, dc * 512:(dc + 1) * 512],
                    lambda dc, par=par: h2_buf[par][:, dc * 512:(dc + 1) * 512],
                    router=False)
            xa_rel[par] = P.cnt["ve"]
            P.wait("sync", "ve", P.cnt["ve"])
            h23 = h2_buf[par].rearrange("p (dc n) -> p dc n", n=512)
            h2_rel[par] = P.inc("sync", f"ht{par}", lambda e, tt=tt, h23=h23: e.dma_start(
                out=hc3[:, :, tt * 512:(tt + 1) * 512], in_=h23), 16)
        P.wait("gpsimd", "ht0", P.cnt["ht0"])
        P.wait("gpsimd", "ht1", P.cnt["ht1"])
        for i in range(4):
            P.inc("gpsimd", "wq", lambda e, i=i: e.dma_start(out=wqkv[i], in_=wq_my[i]), 16)
        P.inc("gpsimd", "wq", lambda e: e.dma_start(out=wqkv[4], in_=wk_my), 16)
        t_wqkv = P.inc("gpsimd", "wq", lambda e: e.dma_start(out=wqkv[5], in_=wv_my), 16)
        P.wait("gpsimd", "ht0", P.cnt["ht0"])
        P.wait("gpsimd", "ht1", P.cnt["ht1"])
        t_cc1 = P.inc("gpsimd", "cc", lambda e: e.collective_compute(
            "AllGather", ALU.bypass,
            replica_groups=[[0, 1, 2, 3], [4, 5, 6, 7]],
            ins=[hc_d.opt()], outs=[hg_d.opt()]))
        P.barrier()

        hg3 = hg_d.rearrange("(blk dc p) n -> p blk dc n", p=128, dc=16)
        hgb_rel = [0, 0]
        vsb_rel = 0
        tpos_rel = 0
        for tt2 in range(16):
            par = tt2 % 2
            blk = tt2 // 4
            coff = (tt2 % 4) * 512
            P.wait("sync", "mm", hgb_rel[par])
            P.wait("sync", "cc", t_cc1)
            hb3 = hg_buf[par].rearrange("p (dc n) -> p dc n", n=512)
            t_hg = P.inc("sync", f"hg{par}", lambda e, hb3=hb3, blk=blk, coff=coff: e.dma_start(
                out=hb3, in_=hg3[:, blk, :, coff:coff + 512]), 16)
            P.wait("gpsimd", "mm", tpos_rel)
            P.inc("gpsimd", "gp", lambda e, tt2=tt2: e.iota(
                tpos, pattern=[[1, 512]], base=tt2 * 512, channel_multiplier=0,
                allow_small_or_imprecise_dtypes=True))
            t_ip = P.cnt["gp"]
            tpos_rel = mmgrp(6, [lambda e, st, sp: e.matmul(
                pb[6][0:64, :], invfq, tpos, start=st, stop=sp)],
                extra_waits=[("gp", t_ip)])
            bread(6, "vector", "ve", lambda e: e.tensor_scalar(
                kf_scr, pb[6][0:64, :], 1.0 / TWO_PI, None, ALU.mult),
                extra_waits=[("ve", P.cnt["ve"])])
            vstep(lambda e: e.tensor_copy(ki_scr, kf_scr))
            vstep(lambda e: e.tensor_copy(kf_scr, ki_scr))
            P.wait("vector", "mm", bmm[6])
            t_r1 = vstep(lambda e: e.scalar_tensor_tensor(
                r_scr, kf_scr, -C1, pb[6][0:64, :], ALU.mult, ALU.add))
            brel[6] = ("ve", t_r1)
            vstep(lambda e: e.scalar_tensor_tensor(
                r_scr, kf_scr, -C2, r_scr, ALU.mult, ALU.add))
            vstep(lambda e: e.scalar_tensor_tensor(
                r_scr, kf_scr, -C3, r_scr, ALU.mult, ALU.add))
            vstep(lambda e: e.tensor_scalar(
                kf_scr, r_scr, math.pi, -TWO_PI, ALU.is_gt, ALU.mult))
            vstep(lambda e: e.tensor_add(r_scr, r_scr, kf_scr))
            vstep(lambda e: e.tensor_scalar(rc_scr, r_scr, math.pi / 2, None, ALU.add))
            vstep(lambda e: e.tensor_scalar(
                kf_scr, rc_scr, math.pi, -TWO_PI, ALU.is_gt, ALU.mult))
            t_rr = vstep(lambda e: e.tensor_add(rc_scr, rc_scr, kf_scr))
            P.wait("scalar", "ve", t_rr)
            P.inc("scalar", "ac", lambda e: e.activation(sin_t[0:64, :], r_scr, AF.Sin, scale=-1.0))
            P.inc("scalar", "ac", lambda e: e.activation(sin_t[64:128, :], r_scr, AF.Sin))
            P.inc("scalar", "ac", lambda e: e.activation(cos_t[0:64, :], rc_scr, AF.Sin))
            t_cs = P.inc("scalar", "ac", lambda e: e.activation(cos_t[64:128, :], rc_scr, AF.Sin))
            P.wait("vector", "ac", t_cs)
            for j in range(6):
                bkq = j % 2
                t_qk = mmgrp(bkq, [
                    (lambda e, st, sp, dc=dc, j=j, par=par, bkq=bkq: e.matmul(
                        pb[bkq], wqkv[j][:, dc * 128:(dc + 1) * 128],
                        hg_buf[par][:, dc * 512:(dc + 1) * 512], start=st, stop=sp))
                    for dc in range(NDC)],
                    extra_waits=[("wq", t_wqkv), (f"hg{par}", t_hg)])
                if j < 5:
                    if j < 4:
                        dest = arQ[:, j * 8192 + tt2 * 512: j * 8192 + tt2 * 512 + 512]
                    else:
                        dest = K_sb[:, tt2 * 512: tt2 * 512 + 512]
                    P.wait("vector", "mm", t_qk)
                    P.wait("vector", "ve", P.cnt["ve"])
                    P.inc("vector", "ve", lambda e, bkq=bkq: e.tensor_mul(rope_a, pb[bkq], cos_t))
                    P.inc("vector", "ve", lambda e, bkq=bkq: e.tensor_mul(
                        rope_b[0:64, :], pb[bkq][64:128, :], sin_t[0:64, :]))
                    P.inc("vector", "ve", lambda e, bkq=bkq: e.tensor_mul(
                        rope_b[64:128, :], pb[bkq][0:64, :], sin_t[64:128, :]))
                    P.wait("vector", "ve", P.cnt["ve"])
                    t_rope = P.inc("vector", "ve", lambda e, dest=dest: e.tensor_add(
                        dest, rope_a, rope_b))
                    brel[bkq] = ("ve", t_rope)
                else:
                    t_vf = bread(bkq, "scalar", "ac", lambda e, bkq=bkq: e.activation(
                        vf_scr, pb[bkq], AF.Copy))
                    P.wait("tensor", "ac", t_vf)
                    P.wait("tensor", "ve", vsb_rel)
                    for ks in range(4):
                        P.inc("tensor", "mm", lambda e, ks=ks: e.transpose(
                            tpb[:, ks * 128:(ks + 1) * 128],
                            vf_scr[:, ks * 128:(ks + 1) * 128], identb))
                    t_tp = P.cnt["mm"]
                    P.wait("vector", "mm", t_tp)
                    vsb_rel = P.inc("vector", "ve", lambda e, tt2=tt2: e.tensor_copy(
                        V_sb[:, tt2 * 512:(tt2 + 1) * 512], tpb[:, 0:512]))
            hgb_rel[par] = P.cnt["mm"]
        P.barrier()

        obk = [5, 6]
        lbk = [2, 3]
        pp_rel = [0, 0, 0, 0]
        obc_rel = [0, 0]
        on_rel = 0
        sweep = 0
        oc3 = oc_d.rearrange("(hq p) n -> p hq n", p=128)
        for hq in range(NQH):
            for qg in range(16):
                sw = sweep % 2
                ob = obk[sw]
                lb = lbk[sw]
                if ob in brel:
                    P.wait("tensor", *brel.pop(ob))
                if lb in brel:
                    P.wait("tensor", *brel.pop(lb))
                nkb = qg + 1
                for kb in range(nkb):
                    for kc in range(4):
                        sb = kc % 2
                        if sb in brel:
                            P.wait("tensor", *brel.pop(sb))
                        t_s = P.inc("tensor", "mm", lambda e, sb=sb, kb=kb, kc=kc, hq=hq, qg=qg: e.matmul(
                            pb[sb], K_sb[:, kb * 512 + kc * 128: kb * 512 + kc * 128 + 128],
                            arQ[:, hq * 8192 + qg * 512: hq * 8192 + qg * 512 + 512],
                            start=True, stop=True))
                        bmm[sb] = t_s
                        t_e = bread(sb, "scalar", "ac", lambda e, sb=sb, kc=kc: e.activation(
                            pprime[kc], pb[sb], AF.Exp, bias=ebias, scale=RS),
                            extra_waits=[("mm", pp_rel[kc])])
                        gate_sem = ("ac", t_e)
                        if kb == qg:
                            P.wait("gpsimd", "ac", t_e)
                            t_gp = P.inc("gpsimd", "gp", lambda e, kc=kc: e.affine_select(
                                out=pprime[kc], in_=pprime[kc], compare_op=ALU.is_ge,
                                fill=0.0, base=-kc * 128, pattern=[[1, 512]],
                                channel_multiplier=-1))
                            gate_sem = ("gp", t_gp)
                        P.wait("tensor", gate_sem[0], gate_sem[1])
                        P.emit("tensor", lambda e, kb=kb, kc=kc, ob=ob, nkb=nkb: e.matmul(
                            pb[ob], V_sb[:, kb * 512 + kc * 128: kb * 512 + kc * 128 + 128],
                            pprime[kc], start=(kb == 0 and kc == 0),
                            stop=(kb == nkb - 1 and kc == 3)))
                        pp_rel[kc] = P.inc("tensor", "mm", lambda e, kc=kc, lb=lb, kb=kb, nkb=nkb: e.matmul(
                            pb[lb][0:1, :], onesb, pprime[kc],
                            start=(kb == 0 and kc == 0), stop=(kb == nkb - 1 and kc == 3)))
                bmm[ob] = bmm[lb] = P.cnt["mm"]
                ir = invl_row[sw]
                t_il = bread(lb, "vector", "ve", lambda e, ir=ir, lb=lb: e.reciprocal(
                    ir, pb[lb][0:1, :]),
                    extra_waits=[("ve", P.cnt["ve"]), ("mm", P.cnt["mm"])])
                P.wait("vector", "ve", t_il)
                t_il2 = P.inc("vector", "ve", lambda e, ir=ir, hq=hq: e.tensor_scalar(
                    ir, ir, hs_sb[0:1, hq:hq + 1], None, ALU.mult))
                mmgrp(4, [lambda e, st, sp, ir=ir: e.matmul(
                    pb[4], onesr, ir, start=st, stop=sp)],
                    extra_waits=[("ve", t_il2)])
                t_bc = bread(4, "vector", "ve", lambda e, sw=sw: e.tensor_copy(
                    obc_sb[sw], pb[4]), extra_waits=[("ve", obc_rel[sw])])
                t_on = bread(ob, "vector", "ve", lambda e, qg=qg, ob=ob, sw=sw: e.tensor_mul(
                    On_sb[:, qg * 512:(qg + 1) * 512], pb[ob], obc_sb[sw]),
                    extra_waits=[("oc", on_rel), ("ve", t_bc)])
                obc_rel[sw] = t_on
                sweep += 1
            P.wait("sync", "ve", P.cnt["ve"])
            on_rel = P.inc("sync", "oc", lambda e, hq=hq: e.dma_start(
                out=oc3[:, hq, :], in_=On_sb), 16)
        P.wait("gpsimd", "oc", P.cnt["oc"])
        P.inc("gpsimd", "cc", lambda e: e.collective_compute(
            "AllGather", ALU.bypass,
            replica_groups=[[0, 1, 2, 3], [4, 5, 6, 7]],
            ins=[oc_d.opt()], outs=[og_d.opt()]))
        P.barrier()

        offs = []

        def setup_regs(e):
            v = e.value_load(qsel_sb[0:1, 0:1])
            r = e.alloc_register("rcb")
            e.reg_alu(r, v, TOK, ALU.mult)
            base = e.snap(r)
            for tt in range(4):
                r2 = e.alloc_register(f"rcb{tt}")
                e.reg_alu(r2, base, tt * 512, ALU.add)
                offs.append(e.snap(r2))
        P.emit("gpsimd", setup_regs)
        arWO = arQ
        for oc in range(NDC):
            P.inc("gpsimd", "wo", lambda e, oc=oc: e.dma_start(
                out=arWO[:, oc * 2048:(oc + 1) * 2048], in_=wo_r[oc]), 16)
        t_wo = P.cnt["wo"]
        ogb_rel = [0, 0]
        x1t_rel = [0, 0]
        xo_rel = [0, 0]
        ogv = og_d.rearrange("(r hq p) n -> p r hq n", p=128, hq=4)
        for tt in range(4):
            par = tt % 2
            P.wait("gpsimd", "mm", ogb_rel[par])
            ob3 = og_buf[par].rearrange("p (hc n) -> p hc n", n=512)
            t_og = P.inc("gpsimd", f"og{par}", lambda e, tt=tt, ob3=ob3: e.dma_start(
                out=ob3.rearrange("p hc n -> p hc n") if False else ob3,
                in_=ogv[:, :, :, bass.ds(offs[tt], 512)].rearrange(
                    "p r hq n -> p (r hq) n")), 16)
            for oc in range(NDC):
                bk = oc % 2
                mmgrp(bk, [
                    (lambda e, st, sp, hc=hc, oc=oc, par=par, bk=bk: e.matmul(
                        pb[bk], arWO[:, oc * 2048 + hc * 128: oc * 2048 + (hc + 1) * 128],
                        og_buf[par][:, hc * 512:(hc + 1) * 512], start=st, stop=sp))
                    for hc in range(NDC)],
                    extra_waits=[(f"og{par}", t_og), ("wo", t_wo)])
                xp = oc % 2
                P.wait("sync", "ve", x1t_rel[xp])
                t_x1 = P.inc("sync", f"x6{xp}", lambda e, oc=oc, tt=tt, xp=xp: e.dma_start(
                    out=x1_t6[xp], in_=x1[oc * 128:(oc + 1) * 128, tt * 512:(tt + 1) * 512]), 16)
                t_add = bread(bk, "vector", "ve", lambda e, xp=xp, bk=bk: e.tensor_add(
                    xo_t6[xp], x1_t6[xp], pb[bk]),
                    extra_waits=[(f"x6{xp}", t_x1), (f"xo{xp}", xo_rel[xp])])
                x1t_rel[xp] = t_add
                P.wait("sync", "ve", t_add)
                xo_rel[xp] = P.inc("sync", f"xo{xp}", lambda e, oc=oc, tt=tt, xp=xp: e.dma_start(
                    out=x2[oc * 128:(oc + 1) * 128, tt * 512:(tt + 1) * 512], in_=xo_t6[xp]), 16)
            ogb_rel[par] = P.cnt["mm"]
        P.barrier()

    if "ffn1" in phases:
        for half in range(2):
            ffn_half(x_in, x1, half)
            P.barrier()
    if "attn" in phases:
        attn()
    if "ffn2" in phases:
        for half in range(2):
            ffn_half(x2, y, half)
            P.barrier()
    P.wait("sync", "xst", P.cnt["xst"])

    P.run()
    stack.close()
    return nc


def _prep_shared(i):
    wg = np.ascontiguousarray(
        i["L0_ffn_w_gate"].reshape(16, 16, 128, 11, 128).transpose(0, 3, 2, 1, 4))
    wu = np.ascontiguousarray(
        i["L0_ffn_w_up"].reshape(16, 16, 128, 11, 128).transpose(0, 3, 2, 1, 4))
    wgu = np.concatenate([wg, wu], axis=4)
    wgu_r = np.ascontiguousarray(wgu.reshape(16, 11, 128, 4096))
    wd_r = np.ascontiguousarray(
        i["L0_ffn_w_down"].reshape(16, 11, 128, 16, 128).transpose(0, 3, 2, 1, 4)
        .reshape(16, 16, 128, 1408))
    shg = i["L0_ffn_sh_gate"].reshape(16, 128, 44, 128).transpose(2, 1, 0, 3)
    shu = i["L0_ffn_sh_up"].reshape(16, 128, 44, 128).transpose(2, 1, 0, 3)
    shgu = np.concatenate([shg, shu], axis=3)
    shgu_r = np.ascontiguousarray(shgu.reshape(44, 128, 4096))
    shd_r = np.ascontiguousarray(
        i["L0_ffn_sh_down"].reshape(44, 128, 16, 128).transpose(2, 1, 0, 3)
        .reshape(16, 128, 5632))
    rt_r = np.ascontiguousarray(
        i["L0_ffn_router_w"].T.reshape(16, 128, 16).transpose(1, 0, 2).reshape(128, 256))
    sgw_r = np.ascontiguousarray(i["L0_ffn_sh_gate_w"].reshape(16, 128).T)
    wo_r = np.ascontiguousarray(
        i["L0_attn_Wo"].reshape(16, 128, 16, 128).transpose(2, 1, 0, 3).reshape(16, 128, 2048))
    return dict(wgu_r=wgu_r, wd_r=wd_r, shgu_r=shgu_r, shd_r=shd_r,
                rt_r=rt_r, sgw_r=sgw_r, wo_r=wo_r)


def _prep_core(i, c, shared):
    s, g = c // 4, c % 4
    x = i["L0_attn_x"]
    x_fm = np.ascontiguousarray(x[s, g * 2048:(g + 1) * 2048, :].T)
    wq = i["L0_attn_Wq"][:, g * 512:(g + 1) * 512]
    wq_my = np.ascontiguousarray(
        wq.reshape(16, 128, 4, 128).transpose(2, 1, 0, 3).reshape(4, 128, 2048))
    wk = i["L0_attn_Wk"][:, g * 128:(g + 1) * 128]
    wk_my = np.ascontiguousarray(
        wk.reshape(16, 128, 128).transpose(1, 0, 2).reshape(128, 2048))
    wv = i["L0_attn_Wv"][:, g * 128:(g + 1) * 128]
    wv_my = np.ascontiguousarray(
        wv.reshape(16, 128, 128).transpose(1, 0, 2).reshape(128, 2048))
    hs_my = np.ascontiguousarray(
        i["L0_attn_head_scale"][0, g * 4:(g + 1) * 4, 0, 0].reshape(1, 4))
    m = dict(shared)
    m.update(x_fm=x_fm, wq_my=wq_my, wk_my=wk_my, wv_my=wv_my, hs_my=hs_my,
             qsel=np.array([[g]], dtype=np.int32))
    return m


def kernel(**inputs):
    shared = _prep_shared(inputs)
    in_maps = [_prep_core(inputs, c, shared) for c in range(8)]
    nc = build_program()
    res = run_bass_kernel_spmd(nc, in_maps, core_ids=list(range(8)))
    out = np.empty((2, 8192, 2048), dtype=np.float32)
    for c in range(8):
        s, g = c // 4, c % 4
        out[s, g * 2048:(g + 1) * 2048, :] = res.results[c]["y"].T
    return out
